# version 02 (final)
# baseline (speedup 1.0000x reference)
import numpy as np
import ml_dtypes
import concourse.bass as bass
import concourse.mybir as mybir
from concourse.bass_utils import run_bass_kernel_spmd
from contextlib import ExitStack

F32 = mybir.dt.float32
BF16 = mybir.dt.bfloat16
AF = mybir.ActivationFunctionType
ALU = mybir.AluOpType
AX = mybir.AxisListType

RC = 1.5 * 2.0 ** 23
B, T, D, I = 8, 4096, 1024, 3072
G = D // 128
IM = I // 128
NT, TS = 8, 512
QMAX = 32767.0
EPS = 1e-5

VENG = {
    "t": ("tensor", False),
    "v": ("vector", False),
    "s": ("scalar", False),
    "g": ("gpsimd", False),
    "q": ("sync", True),
    "sd": ("scalar", True),
    "gd": ("gpsimd", True),
}


class Prog:

    R_STRIPE = 12

    def __init__(self, nc):
        self.nc = nc
        self.entries = []
        self.res = {}
        self.ires = {}

    def _dep_str(self, r, deps, write):
        st = self.res.get(r)
        if st:
            if st["w"] is not None:
                deps.add(st["w"])
            if write:
                deps.update(st["r"])

    @staticmethod
    def _small_write(r):
        if isinstance(r, tuple):
            return r[0] in ("stat", "ccs", "scal") or (r[2] - r[1]) <= 64
        return isinstance(r, str) and r.startswith("consts")

    def emit(self, veng, fn, reads=(), writes=(), tag="", tiny=False):
        if not tiny and writes and all(self._small_write(r) for r in writes):
            tiny = True
        deps = set()
        eid = len(self.entries)
        for r in reads:
            if isinstance(r, str) and r not in self.res and not r.startswith("consts"):
                self.dangling = getattr(self, "dangling", [])
                self.dangling.append((tag, r))
            if isinstance(r, tuple):
                lst = self.ires.setdefault(r[0], [])
                for seg in lst:
                    if seg[0] < r[2] and r[1] < seg[1]:
                        if seg[2]["w"] is not None:
                            deps.add(seg[2]["w"])
                        seg[2]["r"].append(eid)
            else:
                self._dep_str(r, deps, False)
        for r in writes:
            if isinstance(r, tuple):
                lst = self.ires.setdefault(r[0], [])
                keep = []
                for seg in lst:
                    if seg[0] < r[2] and r[1] < seg[1]:
                        if seg[2]["w"] is not None:
                            deps.add(seg[2]["w"])
                        deps.update(seg[2]["r"])
                        if seg[0] >= r[1] and seg[1] <= r[2]:
                            continue
                    keep.append(seg)
                keep.append([r[1], r[2], {"w": eid, "r": []}])
                self.ires[r[0]] = keep
            else:
                self._dep_str(r, deps, True)
        self.entries.append(dict(id=eid, veng=veng, fn=fn, deps=deps, tag=tag, tiny=tiny))
        for r in reads:
            if not isinstance(r, tuple):
                self.res.setdefault(r, {"w": None, "r": []})["r"].append(eid)
        for r in writes:
            if not isinstance(r, tuple):
                self.res[r] = {"w": eid, "r": []}
        return eid

    def finalize(self):
        needed = set()
        for e in self.entries:
            e["waits"] = []
            my_phys = VENG[e["veng"]][0]
            for d in sorted(e["deps"]):
                p = self.entries[d]
                if (VENG[p["veng"]][0] == my_phys and not VENG[p["veng"]][1]
                        and not p["tiny"]):
                    continue
                needed.add(d)
                e["waits"].append(d)
        cnt = {}
        for e in self.entries:
            v = e["veng"]
            if VENG[v][1]:
                i = cnt.get(v, 0)
                cnt[v] = i + 1
                e["sem_key"] = f"{v}{i % self.R_STRIPE}"
                e["inc"] = 16
                e["sem_val"] = 16 * (i // self.R_STRIPE + 1)
            elif e["id"] in needed:
                c = cnt.get(v, 0) + 1
                cnt[v] = c
                e["sem_key"] = v
                e["inc"] = 1
                e["sem_val"] = c
            else:
                e["sem_key"] = None
                e["inc"] = None
        for e in self.entries:
            ws = {}
            for d in e["waits"]:
                p = self.entries[d]
                k = p["sem_key"]
                ws[k] = max(ws.get(k, 0), p["sem_val"])
            e["wait_vals"] = ws
        return cnt

    def run(self, block, sems):
        def body(phys):
            def _(eng):
                hw = {}
                for e in self.entries:
                    if VENG[e["veng"]][0] != phys:
                        continue
                    for v, val in sorted(e["wait_vals"].items()):
                        if hw.get(v, 0) < val:
                            eng.wait_ge(sems[v], val)
                            hw[v] = val
                    instr = e["fn"](eng)
                    if e["inc"] is not None:
                        assert instr is not None, f"entry {e['id']} {e['tag']} needs inc"
                        instr.then_inc(sems[e["sem_key"]], e["inc"])
            return _
        for phys, attr in (("tensor", block.tensor), ("vector", block.vector),
                           ("scalar", block.scalar), ("gpsimd", block.gpsimd),
                           ("sync", block.sync)):
            attr(body(phys))



def build_nc(parts=("attn_a", "ffn1", "attn_b", "ffn2"), n_cores=8, dbg=()):
    nc = bass.Bass("TRN2", target_bir_lowering=False)
    io = {}
    io["x_in"] = nc.dram_tensor("x_in", [T, D], F32, kind="ExternalInput").ap()
    for nm, shp in [
        ("Wq", [D, D]), ("Wk", [D, D]), ("Wv", [D, D]), ("Wg", [D, D]), ("Wo", [D, D]),
        ("gamma", [D]), ("beta", [D]),
        ("W_in", [2 * D, D]), ("conv_w", [D, 5]), ("conv_b", [D]),
        ("a_param", [D]), ("W_out", [D, D]), ("ln_w", [D]), ("ln_b", [D]),
        ("w_gate", [I, D]), ("w_up", [I, D]), ("w_down", [D, I]),
    ]:
        io[nm] = nc.dram_tensor(nm, shp, F32, kind="ExternalInput").ap()
    io["y_out"] = nc.dram_tensor("y_out", [T, D], F32, kind="ExternalOutput").ap()

    hbm = {}
    def dr(name, shape, dt=F32):
        if name in dbg:
            hbm[name] = nc.dram_tensor(name, shape, dt, kind="ExternalOutput")
        else:
            hbm[name] = nc.dram_tensor(name, shape, dt)
        return hbm[name]
    dr("resA", [G, 128, T])
    dr("resB", [G, 128, T])
    dr("k_sp", [G, 128, T])
    dr("zn_sp", [G, 128, T])
    dr("xn_sp", [G, 128, T])
    dr("y_sp", [G, 128, T], BF16)
    dr("up_sp", [IM, 128, T])
    dr("gate_sp", [IM, 128, T])
    dr("x2_sp", [IM, 128, T])
    dr("x3_sp", [IM, 128, T])
    dr("dbg_slab", [NT, 6, 4096])
    dr("dbg_qel", [G, 128, 512])
    dr("dbg_prd", [G, 128, 512])
    dr("dbg_wv", [128, 1024], BF16)
    dr("dbg_bb", [8, 128, 512], BF16)
    dr("w_bounce", [I * D], BF16)
    dr("wupT", [G, IM // 2, 128, 256], BF16)
    dr("wgateT", [G, IM // 2, 128, 256], BF16)
    dr("wdownT", [IM, 128, D], BF16)
    dr("winT", [G, 8, 128, 256], BF16)
    dr("woutT", [G, G // 2, 128, 256], BF16)
    for w in ("wq_bf", "wk_bf", "wv_bf", "wg_bf", "wo_bf"):
        dr(w, [D, D], BF16)
    NBAR = 8
    for i in range(NBAR):
        dr(f"cc_in{i}", [256])
        dr(f"cc_out{i}", [256])

    with ExitStack() as st:
        ec = st.enter_context
        sb = {}
        def sbuf(name, shape, dt=F32, **kw):
            sb[name] = ec(nc.sbuf_tensor(name, shape, dt, **kw))
            return sb[name]
        sbuf("big_bf", [128, G * T], BF16)
        sbuf("wpool", [128, 16384], BF16)
        sbuf("w32", [128, 8192], F32)
        sbuf("s32", [128, 12288], F32)
        sbuf("sbf", [128, 2048], BF16)
        sbuf("consts", [128, 64], F32)
        sbuf("ones_bf", [128, 128], BF16)
        sbuf("identf", [128, 128], F32)
        sbuf("onesr", [1, 128], F32)
        sbuf("sel2g", [2, 128], F32)
        sbuf("sel2c", [128, 2], F32)
        sbuf("scal", [128, 64], F32)
        sbuf("stat", [128, 128], F32)
        sbuf("ccs", [128, 16], F32)
        sbuf("ccr", [1, 512], F32)
        sbuf("znsg_bf", [128, 4096], BF16)
        sbuf("wtr", [128, 3072], BF16, is_transpose_dest=True)
        ps = [ec(nc.psum_tensor(f"pb{i}", [128, 512], F32)) for i in range(8)]
        sems = {}
        for v, (phys, is_dma) in VENG.items():
            if is_dma:
                for i in range(Prog.R_STRIPE):
                    sems[f"{v}{i}"] = ec(nc.semaphore(f"sem_{v}{i}"))
            else:
                sems[v] = ec(nc.semaphore(f"sem_{v}"))
        block = ec(nc.Block())

        P = Prog(nc)
        ctx = dict(nc=nc, P=P, io=io, hbm=hbm, sb=sb, ps=ps, parts=parts,
                   scal_cols={}, nscal=[0], n_cores=n_cores)

        emit_prep(ctx, minimal=not any(p in parts for p in ("ffn1", "ffn2", "attn_a", "attn_b")))
        res_cur = "resA"
        emit_input_transpose(ctx, res_cur)
        bar = [0]
        if "attn_a" in parts:
            res_cur = emit_attn_a(ctx, res_cur)
        if "ffn1" in parts:
            res_cur = emit_ffn(ctx, res_cur, bar)
        if "attn_b" in parts:
            res_cur = emit_attn_b(ctx, res_cur)
        if "ffn2" in parts:
            res_cur = emit_ffn(ctx, res_cur, bar)
        emit_output(ctx, res_cur)

        P.finalize()
        P.run(block, sems)
    return nc



def W32(ctx, lo, size):
    return ctx["sb"]["w32"][:, lo:lo + size], ("w32", lo, lo + size)

def S32(ctx, lo, size):
    return ctx["sb"]["s32"][:, lo:lo + size], ("s32", lo, lo + size)

def BB(ctx, lo, size):
    return ctx["sb"]["big_bf"][:, lo:lo + size], ("big_bf", lo, lo + size)

def WP(ctx, lo, size):
    return ctx["sb"]["wpool"][:, lo:lo + size], ("wpool", lo, lo + size)

def SBF(ctx, lo, size):
    return ctx["sb"]["sbf"][:, lo:lo + size], ("sbf", lo, lo + size)

def STATC(ctx, col, n=1):
    return ctx["sb"]["stat"][:, col:col + n], ("stat", col, col + n)

def CELL(ctx, col):
    return ctx["sb"]["stat"][0:1, col:col + 1], ("stat", col, col + 1)


def emit_prep(ctx, minimal=False):
    P, io, hbm, sb = ctx["P"], ctx["io"], ctx["hbm"], ctx["sb"]

    P.emit("g", lambda e: e.memset(sb["ones_bf"][:, :], 1.0), writes=["ones_bf"], tag="ones")
    P.emit("g", lambda e: e.memset(sb["onesr"][:, :], 1.0), writes=["onesr"], tag="onesr")
    def mk_ident(e):
        e.memset(sb["identf"][:, :], 1.0)
        return e.affine_select(sb["identf"][:, :], sb["identf"][:, :], [[1, 128]],
                               ALU.is_equal, 0.0, base=0, channel_multiplier=-1)
    P.emit("g", mk_ident, writes=["identf"], tag="ident")
    def mk_sel(e):
        e.memset(sb["sel2g"][:, :], 1.0)
        e.affine_select(sb["sel2g"][:, :], sb["sel2g"][:, :], [[1, 128]], ALU.is_ge, 0.0,
                        base=0, channel_multiplier=-64)
        return e.affine_select(sb["sel2g"][:, :], sb["sel2g"][:, :], [[-1, 128]],
                               ALU.is_ge, 0.0, base=63, channel_multiplier=64)
    P.emit("g", mk_sel, writes=["sel2g"], tag="sel2g")
    def mk_selc(e):
        e.memset(sb["sel2c"][:, :], 1.0)
        e.affine_select(sb["sel2c"][:, :], sb["sel2c"][:, :], [[-64, 2]], ALU.is_ge, 0.0,
                        base=0, channel_multiplier=1)
        return e.affine_select(sb["sel2c"][:, :], sb["sel2c"][:, :], [[64, 2]], ALU.is_ge, 0.0,
                               base=63, channel_multiplier=-1)
    P.emit("g", mk_selc, writes=["sel2c"], tag="sel2c")
    if "dbg_bc" in ctx["parts"]:
        P.emit("q", lambda e: e.dma_start(out=hbm["dbg_slab"].ap()[0, 4:5, 0:256].rearrange(
            "a (p c) -> (a p) c", p=2), in_=sb["sel2g"][:, :]),
            reads=["sel2g"], writes=["dbgsel"], tag="dbg_sel2g")

    C = sb["consts"]
    def ld(col, src, tagname):
        P.emit("q", lambda e, d=col, s=src: e.dma_start(
            out=C[:, d:d + G], in_=s.rearrange("(g p) -> p g", p=128),
            allow_slow_non_contiguous=True),
            writes=[f"consts:{tagname}"], tag=f"ld_{tagname}")
    ld(8, io["ln_w"], "ln_w")
    ld(16, io["ln_b"], "ln_b")
    ld(24, io["gamma"], "gamma")
    ld(32, io["beta"], "beta")
    ld(40, io["conv_b"], "conv_b")
    ld(48, io["a_param"], "a_param")
    P.emit("q", lambda e: e.dma_start(
        out=sb["scal"][:, 24:24 + 5 * G].rearrange("p (g k) -> p g k", g=G),
        in_=io["conv_w"].rearrange("(g p) k -> p g k", p=128),
        allow_slow_non_contiguous=True),
        writes=[("scal", 24, 24 + 5 * G)], tag="ld_conv_w")
    P.emit("s", lambda e: e.activation(C[:, 0:G], C[:, 48:48 + G], AF.Tanh),
           reads=["consts:a_param"], writes=["consts:a"], tag="tanh_a")
    P.emit("g", lambda e: e.memset(C[:, 56:57], float(EPS)),
           writes=["consts:eps"], tag="c_eps")
    P.emit("g", lambda e: e.memset(C[:, 57:58], -0.5),
           writes=["consts:neghalf"], tag="c_nh")

    if minimal:
        return

    for nm, dst in [("Wq", "wq_bf"), ("Wk", "wk_bf"), ("Wv", "wv_bf"),
                    ("Wg", "wg_bf"), ("Wo", "wo_bf")]:
        for c in range(8):
            sl, rsl = S32(ctx, 10240, 1024)
            P.emit("q", lambda e, n=nm, cc=c, s=sl: e.dma_start(
                out=s, in_=io[n][128 * cc:128 * (cc + 1), :]),
                writes=[rsl], tag=f"cast_in_{nm}_{c}")
            P.emit("gd", lambda e, n=dst, cc=c, s=sl: e.dma_start(
                out=hbm[n].ap()[128 * cc:128 * (cc + 1), :], in_=s),
                reads=[rsl], writes=[f"{dst}:{c}"], tag=f"cast_out_{nm}_{c}")

    for nm, dstT, rows in [("W_in", "winT", 2 * D), ("W_out", "woutT", D)]:
        nch = rows // 128
        for c in range(nch):
            sl, rsl = S32(ctx, 10240, 1024)
            P.emit("q", lambda e, n=nm, cc=c, s=sl: e.dma_start(
                out=s, in_=io[n][128 * cc:128 * (cc + 1), :]),
                writes=[rsl], tag=f"tb_in_{nm}_{c}")
            P.emit("gd", lambda e, cc=c, s=sl: e.dma_start(
                out=hbm["w_bounce"].ap()[128 * cc * 1024:128 * (cc + 1) * 1024]
                    .rearrange("(p c) -> p c", p=128), in_=s),
                reads=[rsl], writes=[f"bounce:{nm}:{c}"], tag=f"tb_cast_{nm}_{c}")
        for k in range(G):
            P.emit("q", lambda e, kk=k, r=rows: e.dma_start_transpose(
                out=sb["wtr"][:, 0:r],
                in_=hbm["w_bounce"].ap()[0:r * 1024].rearrange("(r c) -> r c", c=1024)
                    [:, 128 * kk:128 * (kk + 1)]),
                reads=[f"bounce:{nm}:{c}" for c in range(nch)], writes=["wtr"],
                tag=f"tr_{nm}_{k}")
            for mp in range(rows // 256):
                P.emit("q", lambda e, kk=k, m=mp, d=dstT: e.dma_start(
                    out=hbm[d].ap()[kk, m], in_=sb["wtr"][:, 256 * m:256 * (m + 1)]),
                    reads=["wtr"], writes=[f"{dstT}:{k}:{mp}"], tag=f"st_{dstT}_{k}_{mp}")

    for nm, dstT, rows, cols in ([] if "noffnprep" in ctx["parts"] else
                                 [("w_up", "wupT", I, D), ("w_gate", "wgateT", I, D),
                                  ("w_down", "wdownT", D, I)]):
        nch = rows // 128
        nblk = cols // 32
        for c in range(nch):
            wr, rwr = W32(ctx, 0, cols)
            dmx, rdmx = W32(ctx, 3072, nblk)
            dsc, rdsc = W32(ctx, 3200, nblk)
            drc, rdrc = W32(ctx, 3328, nblk)
            rq, rrq = W32(ctx, 3456, cols)
            P.emit("q", lambda e, n=nm, cc=c, s=wr: e.dma_start(
                out=s, in_=io[n][128 * cc:128 * (cc + 1), :]),
                writes=[rwr], tag=f"dq_in_{nm}_{c}")
            P.emit("v", lambda e, s=wr, d=dmx, nb=nblk: e.tensor_reduce(
                d, s.rearrange("p (a b) -> p a b", a=nb), AX.X, ALU.max,
                apply_absolute_value=True),
                reads=[rwr], writes=[rdmx], tag=f"dq_max_{nm}_{c}")
            P.emit("v", lambda e, d=dmx, o=dsc: e.tensor_scalar(
                o, d, 1.0 / 7.0, 1e-12, ALU.mult, ALU.max),
                reads=[rdmx], writes=[rdsc], tag=f"dq_s_{nm}_{c}")
            P.emit("v", lambda e, i=dsc, o=drc: e.reciprocal(o, i),
                   reads=[rdsc], writes=[rdrc], tag=f"dq_rcp_{nm}_{c}")
            def dq_mul1(e, s=wr, o=rq, sc=drc, nb=nblk):
                sc3 = bass.AP(sc.tensor, sc.offset, [[sc.tensor.shape[1], 128], [1, nb], [0, 32]])
                return e.tensor_tensor(o.rearrange("p (a b) -> p a b", a=nb),
                                       s.rearrange("p (a b) -> p a b", a=nb), sc3, ALU.mult)
            P.emit("g", dq_mul1, reads=[rwr, rdrc], writes=[rrq], tag=f"dq_m1_{nm}_{c}")
            P.emit("v", lambda e, o=rq: e.tensor_scalar(o, o, RC, RC, ALU.add, ALU.subtract),
                   reads=[rrq], writes=[rrq], tag=f"dq_rnd_{nm}_{c}")
            def dq_mul2(e, s=rq, o=wr, sc=dsc, nb=nblk):
                sc3 = bass.AP(sc.tensor, sc.offset, [[sc.tensor.shape[1], 128], [1, nb], [0, 32]])
                return e.tensor_tensor(o.rearrange("p (a b) -> p a b", a=nb),
                                       s.rearrange("p (a b) -> p a b", a=nb), sc3, ALU.mult)
            P.emit("g", dq_mul2, reads=[rrq, rdsc], writes=[rwr], tag=f"dq_m2_{nm}_{c}")
            P.emit("gd", lambda e, cc=c, s=wr, co=cols: e.dma_start(
                out=hbm["w_bounce"].ap()[128 * cc * co:128 * (cc + 1) * co]
                    .rearrange("(p c) -> p c", p=128), in_=s),
                reads=[rwr], writes=[f"bounce:{nm}:{c}"], tag=f"dq_st_{nm}_{c}")
        nk = cols // 128
        for k in range(nk):
            def tr(e, kk=k, r=rows, co=cols):
                s_ = hbm["w_bounce"].ap()[0:r * co].rearrange("(r c) -> r c", c=co) \
                    [:, 128 * kk:128 * (kk + 1)]
                return e.dma_start_transpose(out=ctx["sb"]["wtr"][:, 0:r], in_=s_)
            P.emit("q", tr, reads=[f"bounce:{nm}:{c}" for c in range(nch)], writes=["wtr"],
                   tag=f"tr_{nm}_{k}")
            if dstT == "wdownT":
                P.emit("q", lambda e, kk=k, d=dstT, r=rows: e.dma_start(
                    out=hbm[d].ap()[kk], in_=ctx["sb"]["wtr"][:, 0:r]),
                    reads=["wtr"], writes=[f"{dstT}:{k}"], tag=f"st_{dstT}_{k}")
            else:
                for mp in range(rows // 256):
                    P.emit("q", lambda e, kk=k, m=mp, d=dstT: e.dma_start(
                        out=hbm[d].ap()[kk, m], in_=ctx["sb"]["wtr"][:, 256 * m:256 * (m + 1)]),
                        reads=["wtr"], writes=[f"{dstT}:{k}:{mp}"], tag=f"st_{dstT}_{k}_{mp}")


def emit_input_transpose(ctx, res):
    P, io, hbm, sb, ps = ctx["P"], ctx["io"], ctx["hbm"], ctx["sb"], ctx["ps"]
    for tb in range(T // 512):
        stg, rstg = S32(ctx, 0, 4096)
        for q in range(4):
            t0 = tb * 512 + q * 128
            xin, rxin = S32(ctx, 4096 + 1024 * (q % 2), 1024)
            P.emit("q", lambda e, t=t0, s=xin: e.dma_start(out=s, in_=io["x_in"][t:t + 128, :]),
                   writes=[rxin], tag=f"it_in_{tb}_{q}")
            for g in range(G):
                bank = (q * 2 + g) % 4
                P.emit("t", lambda e, s=xin, gg=g, b=bank: e.transpose(
                    ps[b][:, 0:128], s[:, 128 * gg:128 * (gg + 1)], sb["identf"][:, :]),
                    reads=[rxin, "identf"], writes=[f"pb{bank}"], tag=f"it_tr_{tb}_{q}_{g}")
                P.emit("s", lambda e, gg=g, qq=q, b=bank, st=stg: e.copy(
                    st[:, 512 * gg + 128 * qq: 512 * gg + 128 * (qq + 1)], ps[b][:, 0:128]),
                    reads=[f"pb{bank}"], writes=[("s32", 512 * g + 128 * q, 512 * g + 128 * (q + 1))],
                    tag=f"it_ev_{tb}_{q}_{g}")
        for g in range(G):
            P.emit("q", lambda e, gg=g, t=tb, st=stg: e.dma_start(
                out=hbm[res].ap()[gg, :, 512 * t:512 * (t + 1)], in_=st[:, 512 * gg:512 * (gg + 1)]),
                reads=[("s32", 512 * g, 512 * (g + 1))], writes=[f"{res}:{g}:{tb}"],
                tag=f"it_out_{tb}_{g}")


def emit_output(ctx, res):
    P, io, hbm, sb, ps = ctx["P"], ctx["io"], ctx["hbm"], ctx["sb"], ctx["ps"]
    for tb in range(T // 512):
        stg, rstg = W32(ctx, 4096, 1024)
        tiles = []
        for g in range(G):
            tl, rtl = W32(ctx, 512 * g, 512)
            tiles.append((tl, rtl))
            P.emit("q", lambda e, gg=g, t=tb, s=tl: e.dma_start(
                out=s, in_=hbm[res].ap()[gg, :, 512 * t:512 * (t + 1)]),
                reads=[f"{res}:{g}:{tb}"], writes=[rtl], tag=f"ot_in_{tb}_{g}")
        for q in range(4):
            for g in range(G):
                bank = 4 + (q * 2 + g) % 4
                P.emit("t", lambda e, gg=g, qq=q, b=bank, s=tiles[g][0]: e.transpose(
                    ps[b][:, 0:128], s[:, 128 * qq:128 * (qq + 1)], sb["identf"][:, :]),
                    reads=[tiles[g][1], "identf"], writes=[f"pb{bank}"],
                    tag=f"ot_tr_{tb}_{q}_{g}")
                P.emit("s", lambda e, gg=g, b=bank, st=stg: e.copy(
                    st[:, 128 * gg:128 * (gg + 1)], ps[b][:, 0:128]),
                    reads=[f"pb{bank}"], writes=[("w32", 4096 + 128 * g, 4096 + 128 * (g + 1))],
                    tag=f"ot_ev_{tb}_{q}_{g}")
            P.emit("q", lambda e, t=tb, qq=q, st=stg: e.dma_start(
                out=io["y_out"][512 * t + 128 * qq:512 * t + 128 * (qq + 1), :], in_=st[:, 0:1024]),
                reads=[("w32", 4096, 5120)], writes=[f"y_out:{tb}:{q}"], tag=f"ot_out_{tb}_{q}")
    P.emit("q", lambda e: e.nop(), reads=[f"y_out:{tb}:{q}" for tb in range(T // 512)
                                          for q in range(4)], tag="final_fence")



def emit_ln_tile(ctx, res_src, tau, sink):
    P, hbm, sb, ps = ctx["P"], ctx["hbm"], ctx["sb"], ctx["ps"]
    rt = []
    xb, rxb = SBF(ctx, 0, 512)
    sq, rsq = SBF(ctx, 512, 512)
    for g in range(G):
        tl, rtl = W32(ctx, 512 * g, 512)
        rt.append((tl, rtl))
        P.emit("q", lambda e, gg=g, s=tl: e.dma_start(
            out=s, in_=hbm[res_src].ap()[gg, :, TS * tau:TS * (tau + 1)]),
            reads=[f"{res_src}:{g}:{tau}"], writes=[rtl], tag=f"ln_in_{res_src}_{tau}_{g}")
    for g in range(G):
        tl, rtl = rt[g]
        P.emit("v", lambda e, s=tl, o=xb: e.tensor_copy(o, s),
               reads=[rtl], writes=[rxb], tag=f"ln_cast_{tau}_{g}")
        P.emit("v", lambda e, i=xb, o=sq: e.tensor_tensor(o, i, i, ALU.mult),
               reads=[rxb], writes=[rsq], tag=f"ln_sq_{tau}_{g}")
        P.emit("t", lambda e, i=xb, gg=g: ctx["nc"].tensor.matmul(
            ps[6][:, :], sb["ones_bf"][:, :], i, start=(gg == 0), stop=(gg == G - 1)),
            reads=[rxb, "ones_bf"], writes=["pb6"], tag=f"ln_mmx_{tau}_{g}")
        P.emit("t", lambda e, i=sq, gg=g: ctx["nc"].tensor.matmul(
            ps[7][:, :], sb["ones_bf"][:, :], i, start=(gg == 0), stop=(gg == G - 1)),
            reads=[rsq, "ones_bf"], writes=["pb7"], tag=f"ln_mmq_{tau}_{g}")
    mu, rmu = W32(ctx, 4096, 512)
    var, rvar = W32(ctx, 4608, 512)
    rstd, rrstd = W32(ctx, 5120, 512)
    P.emit("v", lambda e: ctx["nc"].vector.tensor_scalar(mu, ps[6][:, :], 1.0 / D, None, ALU.mult),
           reads=["pb6"], writes=[rmu], tag=f"ln_mu_{tau}")
    P.emit("v", lambda e: ctx["nc"].vector.tensor_tensor(var, mu, mu, ALU.mult),
           reads=[rmu], writes=[rvar], tag=f"ln_musq_{tau}")
    P.emit("v", lambda e: ctx["nc"].vector.scalar_tensor_tensor(
        var, ps[7][:, :], 1.0 / D, var, ALU.mult, ALU.subtract),
        reads=["pb7", rvar], writes=[rvar], tag=f"ln_var_{tau}")
    C = ctx["sb"]["consts"]
    P.emit("s", lambda e: e.activation(var, var, AF.Ln, bias=C[:, 56:57]),
           reads=[rvar, "consts:eps"], writes=[rvar], tag=f"ln_ln_{tau}")
    P.emit("s", lambda e: e.activation(rstd, var, AF.Exp, scale=C[:, 57:58]),
           reads=[rvar, "consts:neghalf"], writes=[rrstd], tag=f"ln_exp_{tau}")
    xt, rxt = W32(ctx, 5632, 512)
    for g in range(G):
        tl, rtl = rt[g]
        P.emit("v", lambda e, s=tl, o=xt: e.tensor_tensor(o, s, mu, ALU.subtract),
               reads=[rtl, rmu], writes=[rxt], tag=f"ln_sub_{tau}_{g}")
        sink(g, xt, rxt, rstd, rrstd, tl, rtl)


def bcast_scalars(ctx, cells, dst_cols, tag):
    P, sb, ps = ctx["P"], ctx["sb"], ctx["ps"]
    for (c, d) in zip(cells, dst_cols):
        P.emit("t", lambda e, cc=c: ctx["nc"].tensor.matmul(
            ps[6][:, 0:1], sb["onesr"][:, :], sb["stat"][0:1, cc:cc + 1],
            start=True, stop=True),
            reads=[("stat", c, c + 1), "onesr"], writes=["pb6"], tag=f"bc_mm_{tag}_{c}")
        P.emit("v", lambda e, dd=d: e.tensor_copy(sb["scal"][:, dd:dd + 1], ps[6][:, 0:1]),
               reads=["pb6"], writes=[("scal", d, d + 1)], tag=f"bc_cp_{tag}_{d}")


def emit_collective(ctx, bar_idx, src_cols, n_cols, out_cells, tag):
    P, hbm, sb = ctx["P"], ctx["hbm"], ctx["sb"]
    cin, cout = f"cc_in{bar_idx}", f"cc_out{bar_idx}"
    for i, c in enumerate(src_cols):
        P.emit("gd", lambda e, cc=c, ii=i: e.dma_start(
            out=hbm[cin].ap()[128 * ii:128 * (ii + 1)].rearrange("(p c) -> p c", c=1),
            in_=sb["ccs"][:, cc:cc + 1]),
            reads=[("ccs", c, c + 1)], writes=[f"{cin}:{i}"], tag=f"cc_out_{tag}_{i}")
    P.emit("g", lambda e: e.collective_compute(
        "AllReduce", ALU.max, replica_groups=[list(range(ctx["n_cores"]))],
        ins=[hbm[cin].ap().opt()], outs=[hbm[cout].ap().opt()]),
        reads=[f"{cin}:{i}" for i in range(len(src_cols))] + [f"{cin}:pad"],
        writes=[f"{cout}:done"], tag=f"cc_{tag}")
    P.emit("q", lambda e: e.dma_start(
        out=sb["ccr"][:, 0:256], in_=hbm[cout].ap().rearrange("(p c) -> p c", p=1)),
        reads=[f"{cout}:done"], writes=["ccr"], tag=f"cc_in_{tag}")
    for i, oc in enumerate(out_cells):
        P.emit("v", lambda e, ii=i, o=oc: e.tensor_reduce(
            sb["stat"][0:1, o:o + 1], sb["ccr"][0:1, 128 * ii:128 * (ii + 1)],
            AX.X, ALU.max),
            reads=["ccr"], writes=[("stat", oc, oc + 1)], tag=f"cc_red_{tag}_{i}")


def pad_collective_input(ctx, bar_idx, n_used):
    P, hbm, sb = ctx["P"], ctx["hbm"], ctx["sb"]
    cin = f"cc_in{bar_idx}"
    zc, rzc = STATC(ctx, 120, 1)
    P.emit("v", lambda e: e.memset(sb["stat"][:, 120:121], 0.0),
           writes=[rzc], tag=f"ccpad_z_{bar_idx}")
    for i in range(n_used, 2):
        P.emit("gd", lambda e, ii=i: e.dma_start(
            out=hbm[cin].ap()[128 * ii:128 * (ii + 1)].rearrange("(p c) -> p c", c=1),
            in_=sb["stat"][:, 120:121]),
            reads=[rzc], writes=[f"{cin}:pad"], tag=f"ccpad_{bar_idx}_{i}")
    if n_used == 2:
        P.emit("g", lambda e: e.engine_nop(), writes=[f"{cin}:pad"], tag=f"ccpad_nop_{bar_idx}")


def load_w_chunks(ctx, hbm_name, lo, tag):
    P, hbm = ctx["P"], ctx["hbm"]
    regs = []
    for kk in range(8):
        wap, rw = WP(ctx, lo + 1024 * kk, 1024)
        regs.append((wap, rw))
        P.emit("q", lambda e, k=kk, w=wap, n=hbm_name: e.dma_start(
            out=w, in_=hbm[n].ap()[128 * k:128 * (k + 1), :]),
            reads=[f"{hbm_name}:{kk}"], writes=[rw], tag=f"{tag}_{kk}")
    return regs



def emit_attn_a(ctx, res_in):
    P, io, hbm, sb, ps = ctx["P"], ctx["io"], ctx["hbm"], ctx["sb"], ctx["ps"]
    nc = ctx["nc"]
    res_out = "resB" if res_in == "resA" else "resA"

    def bb(g, tau):
        return (sb["big_bf"][:, T * g + TS * tau: T * g + TS * (tau + 1)],
                ("big_bf", T * g + TS * tau, T * g + TS * (tau + 1)))

    def sinkA(tau):
        def sink(g, xt, rxt, rstd, rrstd, tl, rtl):
            dst, rdst = bb(g, tau)
            P.emit("v", lambda e: e.tensor_tensor(dst, xt, rstd, ALU.mult),
                   reads=[rxt, rrstd], writes=[rdst], tag=f"aa_xn_{tau}_{g}")
        return sink
    for tau in range(NT):
        emit_ln_tile(ctx, res_in, tau, sinkA(tau))

    vonly0 = "aa_vonly" in ctx["parts"]
    wk = None if vonly0 else load_w_chunks(ctx, "wk_bf", 0, "aa_ldwk")
    kel, rkel = W32(ctx, 6144, 512)
    kt, rkt = W32(ctx, 6656, 512)
    for tau in range(NT):
        if vonly0:
            break
        for m in range(G):
            bank = (tau * G + m) % 6
            for kk in range(8):
                x_ap, rx = bb(kk, tau)
                P.emit("t", lambda e, w=wk[kk][0], mm=m, x=x_ap, b=bank, k=kk: nc.tensor.matmul(
                    ps[b][:, :], w[:, 128 * mm:128 * (mm + 1)], x,
                    start=(k == 0), stop=(k == 7)),
                    reads=[wk[kk][1], rx], writes=[f"pb{bank}"], tag=f"aa_kmm_{tau}_{m}_{kk}")
            P.emit("s", lambda e, b=bank: e.copy(kel, ps[b][:, :]),
                   reads=[f"pb{bank}"], writes=[rkel], tag=f"aa_kev_{tau}_{m}")
            P.emit("v", lambda e: e.tensor_scalar_min(kt, kel, 0.0),
                   reads=[rkel], writes=[rkt], tag=f"aa_kmin_{tau}_{m}")
            P.emit("s", lambda e: e.activation(kt, kt, AF.Exp),
                   reads=[rkt], writes=[rkt], tag=f"aa_kexp_{tau}_{m}")
            P.emit("v", lambda e: e.scalar_tensor_tensor(kel, kel, 0.0, kt, ALU.max, ALU.add),
                   reads=[rkel, rkt], writes=[rkel], tag=f"aa_kelu_{tau}_{m}")
            P.emit("v", lambda e, mm=m, tt=tau: e.tensor_reduce(
                sb["stat"][:, mm * 8 + tt:mm * 8 + tt + 1], kel, AX.X, ALU.add),
                reads=[rkel], writes=[("stat", m * 8 + tau, m * 8 + tau + 1)],
                tag=f"aa_kred_{tau}_{m}")
            P.emit("sd", lambda e, mm=m, tt=tau: e.dma_start(
                out=hbm["k_sp"].ap()[mm, :, TS * tt:TS * (tt + 1)], in_=kel),
                reads=[rkel], writes=[f"k_sp:{m}:{tau}"], tag=f"aa_ksp_{tau}_{m}")
    for g in range(G):
        if vonly0:
            break
        P.emit("v", lambda e, gg=g: e.tensor_reduce(
            sb["stat"][:, 64 + gg:65 + gg], sb["stat"][:, 8 * gg:8 * gg + 8],
            AX.X, ALU.add),
            reads=[("stat", 8 * g, 8 * g + 8)], writes=[("stat", 64 + g, 65 + g)],
            tag=f"aa_ksum_{g}")

    wq = load_w_chunks(ctx, "wq_bf", 0, "aa_ldwq")
    wv = load_w_chunks(ctx, "wv_bf", 8192, "aa_ldwv")
    if "dbg_slab" in ctx["parts"]:
        for kk in range(8):
            P.emit("q", lambda e, k=kk: e.dma_start(
                out=hbm["dbg_wv"].ap()[0:128, 128 * k:128 * (k + 1)],
                in_=wv[k][0][:, 640:768]),
                reads=[wv[kk][1]], writes=[f"dbgwv:{kk}"], tag=f"dbg_wv_{kk}")
    QEL = 4096
    TMP = 4608
    PRD = 5120
    ZMU = 5632
    ZVR = 6144
    ZRS = 6656
    ZNT = 7168
    QKS = 0
    QSS = 4096
    K16 = 8192
    KLD = 9728
    vonly = "aa_vonly" in ctx["parts"]
    for tau in range(NT):
        if vonly and tau != 1:
            continue
        zt = []
        for g in range(G):
            b0 = (2 * g) % 6
            b1 = g if vonly else (2 * g + 1) % 6
            for kk in range(8):
                if vonly:
                    break
                x_ap, rx = bb(kk, tau)
                P.emit("t", lambda e, w=wq[kk][0], mm=g, x=x_ap, b=b0, k=kk: nc.tensor.matmul(
                    ps[b][:, :], w[:, 128 * mm:128 * (mm + 1)], x,
                    start=(k == 0), stop=(k == 7)),
                    reads=[wq[kk][1], rx], writes=[f"pb{b0}"], tag=f"aa_qmm_{tau}_{g}_{kk}")
            if "dbg_slab" in ctx["parts"] and tau == 1 and g == 0:
                for kk in range(8):
                    x_ap, rx = bb(kk, tau)
                    P.emit("q", lambda e, k2=kk, x=x_ap: e.dma_start(
                        out=hbm["dbg_bb"].ap()[k2], in_=x),
                        reads=[rx], writes=[f"dbgbb:{kk}"], tag=f"dbgbb_{tau}_{kk}")
            for kk in range(8):
                x_ap, rx = bb(kk, tau)
                P.emit("t", lambda e, w=wv[kk][0], mm=g, x=x_ap, b=b1, k=kk: nc.tensor.matmul(
                    ps[b][:, :], w[:, 128 * mm:128 * (mm + 1)], x,
                    start=(k == 0), stop=(k == 7)),
                    reads=[wv[kk][1], rx], writes=[f"pb{b1}"], tag=f"aa_vmm_{tau}_{g}_{kk}")
            qel, rqel = W32(ctx, QEL, 512)
            tmp, rtmp = W32(ctx, TMP, 512)
            vt, rvt = W32(ctx, 512 * g, 512)
            zt.append((vt, rvt))
            P.emit("s", lambda e, b=b0, o=qel: e.copy(o, ps[b][:, :]),
                   reads=[f"pb{b0}"], writes=[rqel], tag=f"aa_qev_{tau}_{g}")
            P.emit("s", lambda e, b=b1, o=vt: e.copy(o, ps[b][:, :]),
                   reads=[f"pb{b1}"], writes=[rvt], tag=f"aa_vev_{tau}_{g}")
            if vonly:
                continue
            if "aa_min1" in ctx["parts"]:
                continue
            P.emit("v", lambda e, a=tmp, b=qel: e.tensor_scalar_min(a, b, 0.0),
                   reads=[rqel], writes=[rtmp], tag=f"aa_qmin_{tau}_{g}")
            P.emit("s", lambda e, a=tmp: e.activation(a, a, AF.Exp),
                   reads=[rtmp], writes=[rtmp], tag=f"aa_qexp_{tau}_{g}")
            P.emit("v", lambda e, a=qel, b=tmp: e.scalar_tensor_tensor(a, a, 0.0, b, ALU.max, ALU.add),
                   reads=[rqel, rtmp], writes=[rqel], tag=f"aa_qelu_{tau}_{g}")
            if "aa_min2" in ctx["parts"]:
                continue
            klt, rklt = S32(ctx, KLD + 512 * (g % 2), 512)
            P.emit("q", lambda e, gg=g, tt=tau, s=klt: e.dma_start(
                out=s, in_=hbm["k_sp"].ap()[gg, :, TS * tt:TS * (tt + 1)]),
                reads=[f"k_sp:{g}:{tau}"], writes=[rklt], tag=f"aa_kld_{tau}_{g}")
            prd, rprd = W32(ctx, PRD, 512)
            if "aa_min4" in ctx["parts"]:
                continue
            P.emit("v", lambda e, s=klt, o=prd, a=qel: e.tensor_tensor(o, a, s, ALU.mult),
                   reads=[rqel, rklt], writes=[rprd], tag=f"aa_prod_{tau}_{g}")
            if "aa_min3" in ctx["parts"] or "aa_min4" in ctx["parts"]:
                continue
            P.emit("t", lambda e, a=prd: nc.tensor.matmul(ps[6][0:2, :], sb["sel2c"][:, :], a,
                                                          start=True, stop=True),
                   reads=[rprd, "sel2c"], writes=["pb6"], tag=f"aa_qkmm_{tau}_{g}")
            P.emit("v", lambda e, gg=g: e.tensor_copy(
                sb["s32"][0:2, QKS + 512 * gg:QKS + 512 * (gg + 1)], ps[6][0:2, :]),
                reads=["pb6"], writes=[("s32", QKS + 512 * g, QKS + 512 * (g + 1))],
                tag=f"aa_qkev_{tau}_{g}", tiny=True)
            P.emit("v", lambda e, gg=g, o=prd, a=qel: e.tensor_scalar_mul(
                o, a, sb["stat"][:, 64 + gg:65 + gg]),
                reads=[rqel, ("stat", 64 + g, 65 + g)], writes=[rprd],
                tag=f"aa_qsp_{tau}_{g}")
            P.emit("t", lambda e, a=prd: nc.tensor.matmul(ps[6][0:2, :], sb["sel2c"][:, :], a,
                                                          start=True, stop=True),
                   reads=[rprd, "sel2c"], writes=["pb6"], tag=f"aa_qsmm_{tau}_{g}")
            P.emit("v", lambda e, gg=g: e.tensor_copy(
                sb["s32"][0:2, QSS + 512 * gg:QSS + 512 * (gg + 1)], ps[6][0:2, :]),
                reads=["pb6"], writes=[("s32", QSS + 512 * g, QSS + 512 * (g + 1))],
                tag=f"aa_qsev_{tau}_{g}", tiny=True)
        if "aa_noz" in ctx["parts"] or vonly:
            continue
        qk_sl = sb["s32"][0:2, QKS:QKS + 4096]
        qs_sl = sb["s32"][0:2, QSS:QSS + 4096]
        rslab = ("s32", QKS, QKS + 4096)
        rqslab = ("s32", QSS, QSS + 4096)
        P.emit("v", lambda e, a=qs_sl: e.tensor_scalar_add(a, a, 1e-6),
               reads=[rqslab], writes=[rqslab], tag=f"aa_qse_{tau}")
        P.emit("s", lambda e, a=qs_sl: e.activation(a, a, AF.Ln),
               reads=[rqslab], writes=[rqslab], tag=f"aa_qln_{tau}")
        P.emit("s", lambda e, a=qs_sl: e.activation(a, a, AF.Exp, scale=-1.0),
               reads=[rqslab], writes=[rqslab], tag=f"aa_qex_{tau}")
        P.emit("v", lambda e, a=qk_sl, b=qs_sl: e.tensor_tensor(a, a, b, ALU.mult),
               reads=[rslab, rqslab], writes=[rslab], tag=f"aa_r_{tau}")
        if "dbg_slab" in ctx["parts"]:
            P.emit("q", lambda e, tt=tau: e.dma_start(
                out=hbm["dbg_slab"].ap()[tt, 0:2, :], in_=sb["s32"][0:2, QKS:QKS + 4096]),
                reads=[rslab], writes=[f"dbgs:{tau}:0"], tag=f"dbg1_{tau}")
            P.emit("q", lambda e, tt=tau: e.dma_start(
                out=hbm["dbg_slab"].ap()[tt, 2:4, :], in_=sb["s32"][0:2, QSS:QSS + 4096]),
                reads=[rqslab], writes=[f"dbgs:{tau}:1"], tag=f"dbg2_{tau}")
        if "aa_nobc" in ctx["parts"]:
            continue
        xbz, rxbz = SBF(ctx, 0, 512)
        sqz, rsqz = SBF(ctx, 512, 512)
        for g in range(G):
            vt, rvt = zt[g]
            b = g % 6
            P.emit("t", lambda e, gg=g, bk=b: nc.tensor.matmul(
                ps[bk][:, :], sb["sel2g"][:, :],
                sb["s32"][0:2, QKS + 512 * gg:QKS + 512 * (gg + 1)],
                start=True, stop=True),
                reads=[("s32", QKS + 512 * g, QKS + 512 * (g + 1)), "sel2g"],
                writes=[f"pb{b}"], tag=f"aa_rbc_{tau}_{g}")
            if "dbg_bc" in ctx["parts"] and tau == 1:
                bcd, rbcd = W32(ctx, ZMU, 512)
                P.emit("v", lambda e, bk=b, o=bcd: e.tensor_copy(o, ps[bk][:, :]),
                       reads=[f"pb{b}"], writes=[rbcd], tag=f"dbgbcv_{tau}_{g}")
                P.emit("q", lambda e, gg=g, s=bcd: e.dma_start(out=hbm["dbg_qel"].ap()[gg], in_=s),
                       reads=[rbcd], writes=[f"dbgbc:{g}"], tag=f"dbgbc_{tau}_{g}")
            P.emit("v", lambda e, bk=b, a=vt: e.tensor_tensor(a, a, ps[bk][:, :], ALU.mult),
                   reads=[rvt, f"pb{b}"], writes=[rvt], tag=f"aa_z_{tau}_{g}")
            if "dbg_z" in ctx["parts"] and tau == 1:
                P.emit("q", lambda e, gg=g, a=vt: e.dma_start(out=hbm["dbg_prd"].ap()[gg], in_=a),
                       reads=[rvt], writes=[f"dbgz:{g}"], tag=f"dbgz_{tau}_{g}")
            P.emit("v", lambda e, o=xbz, a=vt: e.tensor_copy(o, a),
                   reads=[rvt], writes=[rxbz], tag=f"aa_zc_{tau}_{g}")
            P.emit("v", lambda e, o=sqz, a=xbz: e.tensor_tensor(o, a, a, ALU.mult),
                   reads=[rxbz], writes=[rsqz], tag=f"aa_zsq_{tau}_{g}")
            P.emit("t", lambda e, gg=g, a=xbz: nc.tensor.matmul(
                ps[6][:, :], sb["ones_bf"][:, :], a, start=(gg == 0), stop=(gg == G - 1)),
                reads=[rxbz, "ones_bf"], writes=["pb6"], tag=f"aa_zmmx_{tau}_{g}")
            P.emit("t", lambda e, gg=g, a=sqz: nc.tensor.matmul(
                ps[7][:, :], sb["ones_bf"][:, :], a, start=(gg == 0), stop=(gg == G - 1)),
                reads=[rsqz, "ones_bf"], writes=["pb7"], tag=f"aa_zmmq_{tau}_{g}")
        zmu, rzmu = W32(ctx, ZMU, 512)
        zvr, rzvr = W32(ctx, ZVR, 512)
        zrs, rzrs = W32(ctx, ZRS, 512)
        znt, rznt = W32(ctx, ZNT, 512)
        P.emit("v", lambda e, o=zmu: nc.vector.tensor_scalar(o, ps[6][:, :], 1.0 / D, None, ALU.mult),
               reads=["pb6"], writes=[rzmu], tag=f"aa_zmu_{tau}")
        P.emit("v", lambda e, o=zvr, a=zmu: nc.vector.tensor_tensor(o, a, a, ALU.mult),
               reads=[rzmu], writes=[rzvr], tag=f"aa_zms_{tau}")
        P.emit("v", lambda e, o=zvr: nc.vector.scalar_tensor_tensor(
            o, ps[7][:, :], 1.0 / D, o, ALU.mult, ALU.subtract),
            reads=["pb7", rzvr], writes=[rzvr], tag=f"aa_zvar_{tau}")
        P.emit("s", lambda e, a=zvr: e.activation(a, a, AF.Ln, bias=sb["consts"][:, 56:57]),
               reads=[rzvr, "consts:eps"], writes=[rzvr], tag=f"aa_zln_{tau}")
        P.emit("s", lambda e, o=zrs, a=zvr: e.activation(o, a, AF.Exp, scale=sb["consts"][:, 57:58]),
               reads=[rzvr, "consts:neghalf"], writes=[rzrs], tag=f"aa_zexp_{tau}")
        for g in range(G):
            vt, rvt = zt[g]
            P.emit("v", lambda e, o=znt, a=vt, b=zmu: e.tensor_tensor(o, a, b, ALU.subtract),
                   reads=[rvt, rzmu], writes=[rznt], tag=f"aa_zsub_{tau}_{g}")
            P.emit("v", lambda e, o=znt, b=zrs: e.tensor_tensor(o, o, b, ALU.mult),
                   reads=[rznt, rzrs], writes=[rznt], tag=f"aa_zrs_{tau}_{g}")
            P.emit("v", lambda e, gg=g, o=znt: e.tensor_scalar(
                o, o, sb["consts"][:, 24 + gg:25 + gg],
                sb["consts"][:, 32 + gg:33 + gg], ALU.mult, ALU.add),
                reads=[rznt, "consts:gamma", "consts:beta"], writes=[rznt],
                tag=f"aa_zgb_{tau}_{g}")
            P.emit("sd", lambda e, gg=g, tt=tau, a=znt: e.dma_start(
                out=hbm["zn_sp"].ap()[gg, :, TS * tt:TS * (tt + 1)], in_=a),
                reads=[rznt], writes=[f"zn_sp:{g}:{tau}"], tag=f"aa_znsp_{tau}_{g}")

    if vonly:
        return res_out
    wg = load_w_chunks(ctx, "wg_bf", 0, "aa_ldwg")
    wo = load_w_chunks(ctx, "wo_bf", 8192, "aa_ldwo")
    for tau in range(NT):
        for g in range(G):
            b = g % 4
            for kk in range(8):
                x_ap, rx = bb(kk, tau)
                P.emit("t", lambda e, w=wg[kk][0], mm=g, x=x_ap, bk=b, k=kk: nc.tensor.matmul(
                    ps[bk][:, :], w[:, 128 * mm:128 * (mm + 1)], x,
                    start=(k == 0), stop=(k == 7)),
                    reads=[wg[kk][1], rx], writes=[f"pb{b}"], tag=f"aa_gmm_{tau}_{g}_{kk}")
            sg, rsg = W32(ctx, 512 * g, 512)
            P.emit("s", lambda e, bk=b, o=sg: e.activation(o, ps[bk][:, :], AF.Silu),
                   reads=[f"pb{b}"], writes=[rsg], tag=f"aa_gsilu_{tau}_{g}")
            znl, rznl = W32(ctx, 4096 + 512 * (g % 2), 512)
            P.emit("q", lambda e, gg=g, tt=tau, s=znl: e.dma_start(
                out=s, in_=hbm["zn_sp"].ap()[gg, :, TS * tt:TS * (tt + 1)]),
                reads=[f"zn_sp:{g}:{tau}"], writes=[rznl], tag=f"aa_znld_{tau}_{g}")
            dstb = sb["znsg_bf"][:, 512 * g:512 * (g + 1)]
            P.emit("v", lambda e, s=znl, o=dstb, w=sg: e.tensor_tensor(o, s, w, ALU.mult),
                   reads=[rznl, rsg], writes=[("znsg_bf", 512 * g, 512 * (g + 1))],
                   tag=f"aa_znsg_{tau}_{g}")
        for m in range(G):
            b = 4 + m % 4
            for kk in range(8):
                src = sb["znsg_bf"][:, 512 * kk:512 * (kk + 1)]
                P.emit("t", lambda e, w=wo[kk][0], mm=m, x=src, bk=b, k=kk: nc.tensor.matmul(
                    ps[bk][:, :], w[:, 128 * mm:128 * (mm + 1)], x,
                    start=(k == 0), stop=(k == 7)),
                    reads=[wo[kk][1], ("znsg_bf", 512 * kk, 512 * (kk + 1))],
                    writes=[f"pb{b}"], tag=f"aa_omm_{tau}_{m}_{kk}")
            old, rold = S32(ctx, 8192 + 512 * (m % 2), 512)
            P.emit("q", lambda e, mm=m, tt=tau, s=old: e.dma_start(
                out=s, in_=hbm[res_in].ap()[mm, :, TS * tt:TS * (tt + 1)]),
                reads=[f"{res_in}:{m}:{tau}"], writes=[rold], tag=f"aa_old_{tau}_{m}")
            out, rout = S32(ctx, 9216 + 512 * (m % 2), 512)
            P.emit("v", lambda e, s=old, o=out, bk=b: e.tensor_tensor(o, s, ps[bk][:, :], ALU.add),
                   reads=[rold, f"pb{b}"], writes=[rout], tag=f"aa_radd_{tau}_{m}")
            P.emit("sd", lambda e, mm=m, tt=tau, s=out: e.dma_start(
                out=hbm[res_out].ap()[mm, :, TS * tt:TS * (tt + 1)], in_=s),
                reads=[rout], writes=[f"{res_out}:{m}:{tau}"], tag=f"aa_rst_{tau}_{m}")
    return res_out


def emit_ffn(ctx, res_in, bar):
    P, io, hbm, sb, ps = ctx["P"], ctx["io"], ctx["hbm"], ctx["sb"], ctx["ps"]
    nc = ctx["nc"]
    res_out = "resB" if res_in == "resA" else "resA"
    fb = bar[0]
    bar[0] += 4

    def bb(g, tau):
        return (sb["big_bf"][:, T * g + TS * tau: T * g + TS * (tau + 1)],
                ("big_bf", T * g + TS * tau, T * g + TS * (tau + 1)))

    def sink0(tau):
        def sink(g, xt, rxt, rstd, rrstd, tl, rtl):
            xn, rxn = W32(ctx, 6144, 512)
            P.emit("v", lambda e: e.tensor_tensor(xn, xt, rstd, ALU.mult),
                   reads=[rxt, rrstd], writes=[rxn], tag=f"f_xn_{tau}_{g}")
            P.emit("v", lambda e, gg=g, tt=tau: e.tensor_reduce(
                sb["stat"][:, gg * 8 + tt:gg * 8 + tt + 1], xn, AX.X, ALU.max,
                apply_absolute_value=True),
                reads=[rxn], writes=[("stat", g * 8 + tau, g * 8 + tau + 1)],
                tag=f"f_xmax_{tau}_{g}")
            P.emit("sd", lambda e, gg=g, tt=tau: e.dma_start(
                out=hbm["xn_sp"].ap()[gg, :, TS * tt:TS * (tt + 1)], in_=xn),
                reads=[rxn], writes=[f"xn_sp:{g}:{tau}"], tag=f"f_xnsp_{tau}_{g}")
        return sink
    for tau in range(NT):
        emit_ln_tile(ctx, res_in, tau, sink0(tau))
    P.emit("v", lambda e: e.tensor_reduce(sb["ccs"][:, 0:1], sb["stat"][:, 0:64],
                                          AX.X, ALU.max),
           reads=[("stat", 0, 64)], writes=[("ccs", 0, 1)], tag="f_xmaxf")
    pad_collective_input(ctx, fb, 1)
    emit_collective(ctx, fb, [0], 1, [96], f"b{fb}")
    gx, rgx = CELL(ctx, 96)
    sx, rsx = CELL(ctx, 101)
    isx, risx = CELL(ctx, 102)
    P.emit("v", lambda e: e.tensor_scalar(sx, gx, 1.0 / QMAX, 1e-12, ALU.mult, ALU.max),
           reads=[rgx], writes=[rsx], tag="f_sx")
    P.emit("v", lambda e: e.reciprocal(isx, sx), reads=[rsx], writes=[risx], tag="f_isx")
    bcast_scalars(ctx, [102], [0], f"f{fb}_isx")
    if "dbg_ffs" in ctx["parts"] and fb == 0:
        P.emit("q", lambda e: e.dma_start(out=hbm["dbg_qel"].ap()[0][:, 0:128],
                                          in_=sb["stat"][:, :]),
               reads=[("stat", 0, 128)], writes=["dbgst"], tag="dbg_stat")
        P.emit("q", lambda e: e.dma_start(out=hbm["dbg_qel"].ap()[1][:, 0:64],
                                          in_=sb["scal"][:, :]),
               reads=[("scal", 0, 64)], writes=["dbgsc"], tag="dbg_scal")
        P.emit("q", lambda e: e.dma_start(out=hbm["dbg_qel"].ap()[2][0:1, 0:512],
                                          in_=sb["ccr"][:, :]),
               reads=["ccr"], writes=["dbgcc"], tag="dbg_ccr")
    for tau in range(NT):
        for g in range(G):
            xnl, rxnl = S32(ctx, 512 * (g % 2), 512)
            P.emit("q", lambda e, gg=g, tt=tau, s=xnl: e.dma_start(
                out=s, in_=hbm["xn_sp"].ap()[gg, :, TS * tt:TS * (tt + 1)]),
                reads=[f"xn_sp:{g}:{tau}"], writes=[rxnl], tag=f"f_xnld_{tau}_{g}")
            dst, rdst = bb(g, tau)
            P.emit("v", lambda e, s=xnl: e.tensor_scalar(
                s, s, sb["scal"][:, 0:1], RC, ALU.mult, ALU.add),
                reads=[rxnl, ("scal", 0, 1)], writes=[rxnl], tag=f"f_rx1_{tau}_{g}")
            P.emit("v", lambda e, s=xnl, o=dst: e.tensor_scalar(o, s, RC, None, ALU.subtract),
                   reads=[rxnl], writes=[rdst], tag=f"f_rx2_{tau}_{g}")

    for wi, (wname, spname, cbase) in enumerate((("wupT", "up_sp", 0),
                                                 ("wgateT", "gate_sp", 48))):
        for m in range(IM):
            mp, mc = m // 2, (m % 2) * 128
            if m % 2 == 0:
                for kk in range(8):
                    wap, rw = WP(ctx, kk * 512 + 256 * ((mp + wi) % 1), 256)
                    P.emit("q", lambda e, k=kk, w=wap, n=wname, p=mp: e.dma_start(
                        out=w, in_=hbm[n].ap()[k, p]),
                        reads=[f"{wname}:{kk}:{mp}"], writes=[rw], tag=f"f1_w_{wname}_{m}_{kk}")
            for h in (0, 1):
                stg, rstg = S32(ctx, 1024 + 2048 * h, 2048)
                for ti in range(4):
                    tau = h * 4 + ti
                    bank = h * 4 + ti
                    for kk in range(8):
                        x_ap, rx = bb(kk, tau)
                        wap = sb["wpool"][:, kk * 512 + mc:kk * 512 + mc + 128]
                        P.emit("t", lambda e, w=wap, x=x_ap, b=bank, k=kk: nc.tensor.matmul(
                            ps[b][:, :], w, x, start=(k == 0), stop=(k == 7)),
                            reads=[("wpool", kk * 512, kk * 512 + 256), rx],
                            writes=[f"pb{bank}"], tag=f"f1_mm_{wname}_{m}_{tau}_{kk}")
                    P.emit("s", lambda e, b=bank, s=stg, t=ti: e.copy(
                        s[:, 512 * t:512 * (t + 1)], ps[b][:, :]),
                        reads=[f"pb{bank}"],
                        writes=[("s32", 1024 + 2048 * h + 512 * ti, 1024 + 2048 * h + 512 * (ti + 1))],
                        tag=f"f1_ev_{wname}_{m}_{tau}")
                P.emit("v", lambda e, s=stg, c=cbase + m * 2 + h: e.tensor_reduce(
                    sb["stat"][:, c:c + 1], s, AX.X, ALU.max, apply_absolute_value=True),
                    reads=[("s32", 1024 + 2048 * h, 3072 + 2048 * h)],
                    writes=[("stat", cbase + m * 2 + h, cbase + m * 2 + h + 1)],
                    tag=f"f1_max_{wname}_{m}_{h}")
                P.emit("sd", lambda e, s=stg, mm=m, hh=h, n=spname: e.dma_start(
                    out=hbm[n].ap()[mm, :, 2048 * hh:2048 * (hh + 1)], in_=s),
                    reads=[("s32", 1024 + 2048 * h, 3072 + 2048 * h)],
                    writes=[f"{spname}:{m}:{h}"], tag=f"f1_sp_{wname}_{m}_{h}")
    P.emit("v", lambda e: e.tensor_reduce(sb["ccs"][:, 1:2], sb["stat"][:, 0:48],
                                          AX.X, ALU.max),
           reads=[("stat", 0, 48)], writes=[("ccs", 1, 2)], tag="f_upmaxf")
    P.emit("v", lambda e: e.tensor_reduce(sb["ccs"][:, 2:3], sb["stat"][:, 48:96],
                                          AX.X, ALU.max),
           reads=[("stat", 48, 96)], writes=[("ccs", 2, 3)], tag="f_gtmaxf")
    pad_collective_input(ctx, fb + 1, 2)
    emit_collective(ctx, fb + 1, [1, 2], 2, [97, 98], f"b{fb + 1}")
    gup, rgup = CELL(ctx, 97)
    ggt, rggt = CELL(ctx, 98)
    su, rsu = CELL(ctx, 103)
    fu, rfu = CELL(ctx, 104)
    sg, rsg = CELL(ctx, 105)
    fg, rfg = CELL(ctx, 106)
    c2, rc2 = CELL(ctx, 107)
    tmpc, rtmpc = CELL(ctx, 113)
    for (gcell, rgcell, scell, rscell, fcell, rfcell, t) in (
            (gup, rgup, su, rsu, fu, rfu, "u"), (ggt, rggt, sg, rsg, fg, rfg, "g")):
        P.emit("v", lambda e, a=gcell, o=tmpc: e.tensor_tensor(o, a, sx, ALU.mult),
               reads=[rgcell, rsx], writes=[rtmpc], tag=f"f_s1_{t}_{fb}")
        P.emit("v", lambda e, o=scell: e.tensor_scalar(o, tmpc, 1.0 / QMAX, 1e-12,
                                                       ALU.mult, ALU.max),
               reads=[rtmpc], writes=[rscell], tag=f"f_s2_{t}_{fb}")
        P.emit("v", lambda e, o=tmpc, i=scell: e.reciprocal(o, i),
               reads=[rscell], writes=[rtmpc], tag=f"f_s3_{t}_{fb}")
        P.emit("v", lambda e, o=fcell: e.tensor_tensor(o, sx, tmpc, ALU.mult),
               reads=[rsx, rtmpc], writes=[rfcell], tag=f"f_s4_{t}_{fb}")
    P.emit("v", lambda e: e.tensor_scalar(c2, sg, 1.0 / 65536.0, None, ALU.mult),
           reads=[rsg], writes=[rc2], tag=f"f_c2_{fb}")
    bcast_scalars(ctx, [104, 106, 105], [1, 2, 6], f"f{fb}_b1")

    for m in range(IM):
        for u in (0, 1):
            gl, rgl = S32(ctx, 5120 + 2048 * u, 2048)
            P.emit("q", lambda e, mm=m, uu=u, s=gl: e.dma_start(
                out=s, in_=hbm["gate_sp"].ap()[mm, :, 2048 * uu:2048 * (uu + 1)]),
                reads=[f"gate_sp:{m}:{u}"], writes=[rgl], tag=f"f2_gld_{m}_{u}")
            rg, rrg = W32(ctx, 0, 2048)
            sig, rsig = W32(ctx, 2048, 2048)
            r2, rr2 = W32(ctx, 4096, 2048)
            x2, rx2 = W32(ctx, 6144, 2048)
            P.emit("v", lambda e, s=gl, o=rg: e.tensor_scalar(
                o, s, sb["scal"][:, 2:3], RC, ALU.mult, ALU.add),
                reads=[rgl, ("scal", 2, 3)], writes=[rrg], tag=f"f2_rg1_{m}_{u}")
            P.emit("v", lambda e, o=rg: e.tensor_scalar(o, o, RC, None, ALU.subtract),
                   reads=[rrg], writes=[rrg], tag=f"f2_rg2_{m}_{u}")
            P.emit("s", lambda e, o=sig, i=rg: e.activation(
                o, i, AF.Sigmoid, scale=sb["scal"][:, 6:7]),
                reads=[rrg, ("scal", 6, 7)], writes=[rsig], tag=f"f2_sig_{m}_{u}")
            P.emit("v", lambda e, o=r2, i=sig: e.tensor_scalar(
                o, i, 65536.0, RC, ALU.mult, ALU.add),
                reads=[rsig], writes=[rr2], tag=f"f2_r21_{m}_{u}")
            P.emit("v", lambda e, o=r2: e.tensor_scalar(
                o, o, RC, 65535.0, ALU.subtract, ALU.min),
                reads=[rr2], writes=[rr2], tag=f"f2_r22_{m}_{u}")
            P.emit("v", lambda e, o=x2, a=rg, b=r2: e.tensor_tensor(o, a, b, ALU.mult),
                   reads=[rrg, rr2], writes=[rx2], tag=f"f2_x2_{m}_{u}")
            P.emit("v", lambda e, o=x2, c=m * 2 + u: e.tensor_reduce(
                sb["stat"][:, c:c + 1], o, AX.X, ALU.max, apply_absolute_value=True),
                reads=[rx2], writes=[("stat", m * 2 + u, m * 2 + u + 1)],
                tag=f"f2_max_{m}_{u}")
            P.emit("sd", lambda e, mm=m, uu=u, s=x2: e.dma_start(
                out=hbm["x2_sp"].ap()[mm, :, 2048 * uu:2048 * (uu + 1)], in_=s),
                reads=[rx2], writes=[f"x2_sp:{m}:{u}"], tag=f"f2_sp_{m}_{u}")
    P.emit("v", lambda e: e.tensor_reduce(sb["ccs"][:, 3:4], sb["stat"][:, 0:48],
                                          AX.X, ALU.max),
           reads=[("stat", 0, 48)], writes=[("ccs", 3, 4)], tag="f_x2maxf")
    pad_collective_input(ctx, fb + 2, 1)
    emit_collective(ctx, fb + 2, [3], 1, [99], f"b{fb + 2}")
    gx2, rgx2 = CELL(ctx, 99)
    s2, rs2 = CELL(ctx, 108)
    f2, rf2 = CELL(ctx, 109)
    c3, rc3 = CELL(ctx, 110)
    P.emit("v", lambda e: e.tensor_tensor(tmpc, gx2, c2, ALU.mult),
           reads=[rgx2, rc2], writes=[rtmpc], tag=f"f_s5_{fb}")
    P.emit("v", lambda e: e.tensor_scalar(s2, tmpc, 1.0 / QMAX, 1e-12, ALU.mult, ALU.max),
           reads=[rtmpc], writes=[rs2], tag=f"f_s6_{fb}")
    P.emit("v", lambda e: e.reciprocal(tmpc, s2), reads=[rs2], writes=[rtmpc], tag=f"f_s7_{fb}")
    P.emit("v", lambda e: e.tensor_tensor(f2, c2, tmpc, ALU.mult),
           reads=[rc2, rtmpc], writes=[rf2], tag=f"f_s8_{fb}")
    P.emit("v", lambda e: e.tensor_tensor(c3, s2, su, ALU.mult),
           reads=[rs2, rsu], writes=[rc3], tag=f"f_s9_{fb}")
    bcast_scalars(ctx, [109], [3], f"f{fb}_b2")

    for m in range(IM):
        for u in (0, 1):
            x2l, rx2l = S32(ctx, 5120 + 2048 * u, 2048)
            upl, rupl = S32(ctx, 9216 + 1024 * u, 2048) if False else S32(ctx, 1024 + 2048 * u, 2048)
            P.emit("q", lambda e, mm=m, uu=u, s=x2l: e.dma_start(
                out=s, in_=hbm["x2_sp"].ap()[mm, :, 2048 * uu:2048 * (uu + 1)]),
                reads=[f"x2_sp:{m}:{u}"], writes=[rx2l], tag=f"f3_xld_{m}_{u}")
            P.emit("q", lambda e, mm=m, uu=u, s=upl: e.dma_start(
                out=s, in_=hbm["up_sp"].ap()[mm, :, 2048 * uu:2048 * (uu + 1)]),
                reads=[f"up_sp:{m}:{u}"], writes=[rupl], tag=f"f3_uld_{m}_{u}")
            rg2, rrg2 = W32(ctx, 0, 2048)
            ru, rru = W32(ctx, 2048, 2048)
            x3, rx3 = W32(ctx, 4096, 2048)
            P.emit("v", lambda e, s=x2l, o=rg2: e.tensor_scalar(
                o, s, sb["scal"][:, 3:4], RC, ALU.mult, ALU.add),
                reads=[rx2l, ("scal", 3, 4)], writes=[rrg2], tag=f"f3_g21_{m}_{u}")
            P.emit("v", lambda e, o=rg2: e.tensor_scalar(o, o, RC, None, ALU.subtract),
                   reads=[rrg2], writes=[rrg2], tag=f"f3_g22_{m}_{u}")
            P.emit("v", lambda e, s=upl, o=ru: e.tensor_scalar(
                o, s, sb["scal"][:, 1:2], RC, ALU.mult, ALU.add),
                reads=[rupl, ("scal", 1, 2)], writes=[rru], tag=f"f3_ru1_{m}_{u}")
            P.emit("v", lambda e, o=ru: e.tensor_scalar(o, o, RC, None, ALU.subtract),
                   reads=[rru], writes=[rru], tag=f"f3_ru2_{m}_{u}")
            P.emit("v", lambda e, o=x3, a=rg2, b=ru: e.tensor_tensor(o, a, b, ALU.mult),
                   reads=[rrg2, rru], writes=[rx3], tag=f"f3_x3_{m}_{u}")
            P.emit("v", lambda e, o=x3, c=48 + m * 2 + u: e.tensor_reduce(
                sb["stat"][:, c:c + 1], o, AX.X, ALU.max, apply_absolute_value=True),
                reads=[rx3], writes=[("stat", 48 + m * 2 + u, 48 + m * 2 + u + 1)],
                tag=f"f3_max_{m}_{u}")
            P.emit("sd", lambda e, mm=m, uu=u, s=x3: e.dma_start(
                out=hbm["x3_sp"].ap()[mm, :, 2048 * uu:2048 * (uu + 1)], in_=s),
                reads=[rx3], writes=[f"x3_sp:{m}:{u}"], tag=f"f3_sp_{m}_{u}")
    P.emit("v", lambda e: e.tensor_reduce(sb["ccs"][:, 4:5], sb["stat"][:, 48:96],
                                          AX.X, ALU.max),
           reads=[("stat", 48, 96)], writes=[("ccs", 4, 5)], tag="f_x3maxf")
    pad_collective_input(ctx, fb + 3, 1)
    emit_collective(ctx, fb + 3, [4], 1, [100], f"b{fb + 3}")
    gx3, rgx3 = CELL(ctx, 100)
    so, rso = CELL(ctx, 111)
    f3c, rf3c = CELL(ctx, 112)
    P.emit("v", lambda e: e.tensor_tensor(tmpc, gx3, c3, ALU.mult),
           reads=[rgx3, rc3], writes=[rtmpc], tag=f"f_sa_{fb}")
    P.emit("v", lambda e: e.tensor_scalar(so, tmpc, 1.0 / QMAX, 1e-12, ALU.mult, ALU.max),
           reads=[rtmpc], writes=[rso], tag=f"f_sb_{fb}")
    P.emit("v", lambda e: e.reciprocal(tmpc, so), reads=[rso], writes=[rtmpc], tag=f"f_sc_{fb}")
    P.emit("v", lambda e: e.tensor_tensor(f3c, c3, tmpc, ALU.mult),
           reads=[rc3, rtmpc], writes=[rf3c], tag=f"f_sd_{fb}")
    bcast_scalars(ctx, [112, 111], [4, 5], f"f{fb}_b3")
    if "dbg_ffs" in ctx["parts"] and fb == 0:
        P.emit("q", lambda e: e.dma_start(out=hbm["dbg_qel"].ap()[3][:, 0:128],
                                          in_=sb["stat"][:, :]),
               reads=[("stat", 0, 128)], writes=["dbgst3"], tag="dbg_stat3")
        P.emit("q", lambda e: e.dma_start(out=hbm["dbg_qel"].ap()[4][:, 0:64],
                                          in_=sb["scal"][:, :]),
               reads=[("scal", 0, 64)], writes=["dbgsc3"], tag="dbg_scal3")

    for tau in range(NT):
        for kk in range(IM):
            x3l, rx3l = S32(ctx, 512 * (kk % 2), 512)
            P.emit("q", lambda e, k=kk, tt=tau, s=x3l: e.dma_start(
                out=s, in_=hbm["x3_sp"].ap()[k, :, TS * tt:TS * (tt + 1)]),
                reads=[f"x3_sp:{kk}:{tau // 4}"], writes=[rx3l], tag=f"f4_xld_{tau}_{kk}")
            P.emit("v", lambda e, s=x3l: e.tensor_scalar(
                s, s, sb["scal"][:, 4:5], RC, ALU.mult, ALU.add),
                reads=[rx3l, ("scal", 4, 5)], writes=[rx3l], tag=f"f4_ro1_{tau}_{kk}")
            rob = sb["big_bf"][:, 512 * kk:512 * (kk + 1)]
            rrob = ("big_bf", 512 * kk, 512 * (kk + 1))
            rol = sb["big_bf"][:, 16384 + 512 * kk:16384 + 512 * (kk + 1)]
            rrol = ("big_bf", 16384 + 512 * kk, 16384 + 512 * (kk + 1))
            hif, rhif = W32(ctx, 6144 + 512 * (kk % 2), 512)
            P.emit("v", lambda e, s=x3l, o=rob: e.tensor_scalar(o, s, RC, None, ALU.subtract),
                   reads=[rx3l], writes=[rrob], tag=f"f4_ro2_{tau}_{kk}")
            P.emit("v", lambda e, s=x3l, o=hif: e.tensor_scalar(o, s, RC, None, ALU.subtract),
                   reads=[rx3l], writes=[rhif], tag=f"f4_ro2b_{tau}_{kk}")
            P.emit("v", lambda e, a=hif, b2=rob, o=rol: e.tensor_tensor(o, a, b2, ALU.subtract),
                   reads=[rhif, rrob], writes=[rrol], tag=f"f4_lo_{tau}_{kk}")
            wdp, rwdp = WP(ctx, 1024 * (kk % 4), 1024)
            P.emit("q", lambda e, k=kk, w=wdp: e.dma_start(out=w, in_=hbm["wdownT"].ap()[k]),
                   reads=[f"wdownT:{kk}"], writes=[rwdp], tag=f"f4_w_{tau}_{kk}")
            for m in range(G):
                P.emit("t", lambda e, w=wdp, mm=m, x=rob, b=m, k=kk: nc.tensor.matmul(
                    ps[b][:, :], w[:, 128 * mm:128 * (mm + 1)], x,
                    start=(k == 0), stop=False),
                    reads=[rwdp, rrob], writes=[f"pb{m}"], tag=f"f4_mm_{tau}_{kk}_{m}")
                P.emit("t", lambda e, w=wdp, mm=m, x=rol, b=m, k=kk: nc.tensor.matmul(
                    ps[b][:, :], w[:, 128 * mm:128 * (mm + 1)], x,
                    start=False, stop=(k == IM - 1)),
                    reads=[rwdp, rrol], writes=[f"pb{m}"], tag=f"f4_mmlo_{tau}_{kk}_{m}")
        for m in range(G):
            old, rold = S32(ctx, 1024 + 512 * (m % 2), 512)
            P.emit("q", lambda e, mm=m, tt=tau, s=old: e.dma_start(
                out=s, in_=hbm[res_in].ap()[mm, :, TS * tt:TS * (tt + 1)]),
                reads=[f"{res_in}:{m}:{tau}"], writes=[rold], tag=f"f4_old_{tau}_{m}")
            out, rout = S32(ctx, 2048 + 512 * (m % 2), 512)
            P.emit("v", lambda e, s=old, o=out, b=m: e.scalar_tensor_tensor(
                o, ps[b][:, :], sb["scal"][:, 5:6], s, ALU.mult, ALU.add),
                reads=[rold, f"pb{m}", ("scal", 5, 6)], writes=[rout],
                tag=f"f4_radd_{tau}_{m}")
            P.emit("sd", lambda e, mm=m, tt=tau, s=out: e.dma_start(
                out=hbm[res_out].ap()[mm, :, TS * tt:TS * (tt + 1)], in_=s),
                reads=[rout], writes=[f"{res_out}:{m}:{tau}"], tag=f"f4_rst_{tau}_{m}")
    return res_out


def emit_attn_b(ctx, res_in):
    P, io, hbm, sb, ps = ctx["P"], ctx["io"], ctx["hbm"], ctx["sb"], ctx["ps"]
    nc = ctx["nc"]
    res_out = "resB" if res_in == "resA" else "resA"

    def bb(g, tau):
        return (sb["big_bf"][:, T * g + TS * tau: T * g + TS * (tau + 1)],
                ("big_bf", T * g + TS * tau, T * g + TS * (tau + 1)))

    def sink3(tau):
        def sink(g, xt, rxt, rstd, rrstd, tl, rtl):
            xn, rxn = W32(ctx, 6144, 512)
            P.emit("v", lambda e: e.tensor_tensor(xn, xt, rstd, ALU.mult),
                   reads=[rxt, rrstd], writes=[rxn], tag=f"b_xn_{tau}_{g}")
            dst, rdst = bb(g, tau)
            P.emit("v", lambda e: e.tensor_copy(dst, xn),
                   reads=[rxn], writes=[rdst], tag=f"b_xnb_{tau}_{g}")
            P.emit("sd", lambda e, gg=g, tt=tau: e.dma_start(
                out=hbm["xn_sp"].ap()[gg, :, TS * tt:TS * (tt + 1)], in_=xn),
                reads=[rxn], writes=[f"xn_sp:{g}:{tau}"], tag=f"b_xnsp_{tau}_{g}")
        return sink
    for tau in range(NT):
        emit_ln_tile(ctx, res_in, tau, sink3(tau))

    VP = 4096
    for g in range(G):
        if g % 2 == 0:
            for kk in range(8):
                wgp, rwgp = WP(ctx, kk * 1024, 256)
                wvp, rwvp = WP(ctx, kk * 1024 + 512, 256)
                P.emit("q", lambda e, k=kk, w=wgp, p=g // 2: e.dma_start(
                    out=w, in_=hbm["winT"].ap()[k, p]),
                    reads=[f"winT:{kk}:{g // 2}"], writes=[rwgp], tag=f"b_wg_{g}_{kk}")
                P.emit("q", lambda e, k=kk, w=wvp, p=4 + g // 2: e.dma_start(
                    out=w, in_=hbm["winT"].ap()[k, p]),
                    reads=[f"winT:{kk}:{4 + g // 2}"], writes=[rwvp], tag=f"b_wv_{g}_{kk}")
        mc = (g % 2) * 128
        for tau in range(NT):
            bank = tau % 4
            for kk in range(8):
                x_ap, rx = bb(kk, tau)
                wap = sb["wpool"][:, kk * 1024 + mc:kk * 1024 + mc + 128]
                P.emit("t", lambda e, w=wap, x=x_ap, b=bank, k=kk: nc.tensor.matmul(
                    ps[b][:, :], w, x, start=(k == 0), stop=(k == 7)),
                    reads=[("wpool", kk * 1024, kk * 1024 + 256), rx],
                    writes=[f"pb{bank}"], tag=f"b_gmm_{g}_{tau}_{kk}")
            P.emit("s", lambda e, b=bank, tt=tau: e.activation(
                sb["s32"][:, 512 * tt:512 * (tt + 1)], ps[b][:, :], AF.Sigmoid),
                reads=[f"pb{bank}"], writes=[("s32", 512 * tau, 512 * (tau + 1))],
                tag=f"b_gev_{g}_{tau}")
        P.emit("v", lambda e: e.memset(sb["w32"][:, 0:2], 0.0),
               writes=[("w32", 0, 2)], tag=f"b_vz1_{g}")
        P.emit("v", lambda e: e.memset(sb["w32"][:, 4098:4104], 0.0),
               writes=[("w32", 4098, 4104)], tag=f"b_vz2_{g}")
        for tau in range(NT):
            bank = 4 + tau % 4
            for kk in range(8):
                x_ap, rx = bb(kk, tau)
                wap = sb["wpool"][:, kk * 1024 + 512 + mc:kk * 1024 + 512 + mc + 128]
                P.emit("t", lambda e, w=wap, x=x_ap, b=bank, k=kk: nc.tensor.matmul(
                    ps[b][:, :], w, x, start=(k == 0), stop=(k == 7)),
                    reads=[("wpool", kk * 1024 + 512, kk * 1024 + 768), rx],
                    writes=[f"pb{bank}"], tag=f"b_vmm_{g}_{tau}_{kk}")
            P.emit("s", lambda e, b=bank, tt=tau: e.copy(
                sb["w32"][:, 2 + 512 * tt:2 + 512 * (tt + 1)], ps[b][:, :]),
                reads=[f"pb{bank}"], writes=[("w32", 2 + 512 * tau, 2 + 512 * (tau + 1))],
                tag=f"b_vev_{g}_{tau}")
        acc = sb["s32"][:, 4096:8192]
        racc = ("s32", 4096, 8192)
        vp = sb["w32"]
        rvp = ("w32", 0, 4104)
        P.emit("v", lambda e, gg=g, o=acc, a=vp: e.tensor_scalar_mul(
            o, a[:, 0:4096], sb["scal"][:, 24 + gg * 5:25 + gg * 5]),
            reads=[rvp, ("scal", 24 + g * 5, 25 + g * 5)], writes=[racc],
            tag=f"b_c0_{g}")
        for k in range(1, 5):
            P.emit("v", lambda e, gg=g, kt=k, o=acc, a=vp: e.scalar_tensor_tensor(
                o, a[:, kt:kt + 4096], sb["scal"][:, 24 + gg * 5 + kt:25 + gg * 5 + kt],
                o, ALU.mult, ALU.add),
                reads=[rvp, racc, ("scal", 24 + g * 5 + k, 25 + g * 5 + k)],
                writes=[racc], tag=f"b_c{k}_{g}")
        P.emit("v", lambda e, gg=g, o=acc: e.tensor_scalar_add(
            o, o, sb["consts"][:, 40 + gg:41 + gg]),
            reads=[racc, "consts:conv_b"], writes=[racc], tag=f"b_cb_{g}")
        P.emit("v", lambda e, o=acc: e.tensor_tensor(o, o, sb["s32"][:, 0:4096], ALU.mult),
               reads=[racc, ("s32", 0, 4096)], writes=[racc], tag=f"b_gv_{g}")
        arep = sb["s32"][:, 8192:12288]
        rarep = ("s32", 8192, 12288)
        P.emit("v", lambda e, gg=g, o=arep, a=acc: e.tensor_scalar(
            o, a, 0.0, sb["consts"][:, 0 + gg:1 + gg], ALU.mult, ALU.add),
            reads=[racc, "consts:a"], writes=[rarep], tag=f"b_arep_{g}")
        P.emit("v", lambda e, a=arep, b=acc: e.tensor_tensor_scan(
            sb["znsg_bf"][:, :], a, b, 0.0, ALU.mult, ALU.add),
            reads=[rarep, racc], writes=[("znsg_bf", 0, 4096)], tag=f"b_scan_{g}")
        P.emit("sd", lambda e, gg=g: e.dma_start(
            out=hbm["y_sp"].ap()[gg], in_=sb["znsg_bf"][:, :]),
            reads=[("znsg_bf", 0, 4096)], writes=[f"y_sp:{g}"], tag=f"b_ysp_{g}")

    for g in range(G):
        dst, rdst = (sb["big_bf"][:, T * g:T * (g + 1)], ("big_bf", T * g, T * (g + 1)))
        P.emit("q", lambda e, gg=g, o=dst: e.dma_start(out=o, in_=hbm["y_sp"].ap()[gg]),
               reads=[f"y_sp:{g}"], writes=[rdst], tag=f"b_yld_{g}")
    for kk in range(G):
        for mp in range(4):
            wap, rw = WP(ctx, kk * 1024 + mp * 256, 256)
            P.emit("q", lambda e, k=kk, p=mp, w=wap: e.dma_start(
                out=w, in_=hbm["woutT"].ap()[k, p]),
                reads=[f"woutT:{kk}:{mp}"], writes=[rw], tag=f"b_wo_{kk}_{mp}")
    xbz, rxbz = SBF(ctx, 0, 512)
    sqz, rsqz = SBF(ctx, 512, 512)
    for tau in range(NT):
        rts = []
        for m in range(G):
            bank = m
            for kk in range(G):
                x_ap, rx = bb(kk, tau)
                wap = sb["wpool"][:, kk * 1024 + m * 128:kk * 1024 + m * 128 + 128]
                P.emit("t", lambda e, w=wap, x=x_ap, b=bank, k=kk: nc.tensor.matmul(
                    ps[b][:, :], w, x, start=(k == 0), stop=(k == 7)),
                    reads=[("wpool", kk * 1024, kk * 1024 + 1024), rx],
                    writes=[f"pb{bank}"], tag=f"b_omm_{tau}_{m}_{kk}")
            xnl, rxnl = S32(ctx, 512 * (m % 2), 512)
            P.emit("q", lambda e, mm=m, tt=tau, s=xnl: e.dma_start(
                out=s, in_=hbm["xn_sp"].ap()[mm, :, TS * tt:TS * (tt + 1)]),
                reads=[f"xn_sp:{m}:{tau}"], writes=[rxnl], tag=f"b_xnld_{tau}_{m}")
            ri, rri = W32(ctx, 512 * m, 512)
            rts.append((ri, rri))
            P.emit("v", lambda e, s=xnl, o=ri, b=bank: e.tensor_tensor(
                o, s, ps[b][:, :], ALU.add),
                reads=[rxnl, f"pb{bank}"], writes=[rri], tag=f"b_ri_{tau}_{m}")
        for m in range(G):
            ri, rri = rts[m]
            P.emit("v", lambda e, s=ri: e.tensor_copy(xbz, s),
                   reads=[rri], writes=[rxbz], tag=f"b_lc_{tau}_{m}")
            P.emit("v", lambda e: e.tensor_tensor(sqz, xbz, xbz, ALU.mult),
                   reads=[rxbz], writes=[rsqz], tag=f"b_lsq_{tau}_{m}")
            P.emit("t", lambda e, gg=m: nc.tensor.matmul(
                ps[6][:, :], sb["ones_bf"][:, :], xbz, start=(gg == 0), stop=(gg == G - 1)),
                reads=[rxbz, "ones_bf"], writes=["pb6"], tag=f"b_lmmx_{tau}_{m}")
            P.emit("t", lambda e, gg=m: nc.tensor.matmul(
                ps[7][:, :], sb["ones_bf"][:, :], sqz, start=(gg == 0), stop=(gg == G - 1)),
                reads=[rsqz, "ones_bf"], writes=["pb7"], tag=f"b_lmmq_{tau}_{m}")
        mu, rmu = W32(ctx, 4096, 512)
        var, rvar = W32(ctx, 4608, 512)
        rstd, rrstd = W32(ctx, 5120, 512)
        tmp, rtmp = W32(ctx, 5632, 512)
        P.emit("v", lambda e, o=mu: nc.vector.tensor_scalar(o, ps[6][:, :], 1.0 / D, None, ALU.mult),
               reads=["pb6"], writes=[rmu], tag=f"b_lmu_{tau}")
        P.emit("v", lambda e, o=var, a=mu: nc.vector.tensor_tensor(o, a, a, ALU.mult),
               reads=[rmu], writes=[rvar], tag=f"b_lms_{tau}")
        P.emit("v", lambda e, o=var: nc.vector.scalar_tensor_tensor(
            o, ps[7][:, :], 1.0 / D, o, ALU.mult, ALU.subtract),
            reads=["pb7", rvar], writes=[rvar], tag=f"b_lvar_{tau}")
        P.emit("s", lambda e, a=var: e.activation(a, a, AF.Ln, bias=sb["consts"][:, 56:57]),
               reads=[rvar, "consts:eps"], writes=[rvar], tag=f"b_lln_{tau}")
        P.emit("s", lambda e, o=rstd, a=var: e.activation(o, a, AF.Exp, scale=sb["consts"][:, 57:58]),
               reads=[rvar, "consts:neghalf"], writes=[rrstd], tag=f"b_lexp_{tau}")
        for m in range(G):
            ri, rri = rts[m]
            P.emit("v", lambda e, s=ri, o=tmp, b2=mu: e.tensor_tensor(o, s, b2, ALU.subtract),
                   reads=[rri, rmu], writes=[rtmp], tag=f"b_lsub_{tau}_{m}")
            P.emit("v", lambda e, o=tmp, b2=rstd: e.tensor_tensor(o, o, b2, ALU.mult),
                   reads=[rtmp, rrstd], writes=[rtmp], tag=f"b_lrs_{tau}_{m}")
            P.emit("v", lambda e, gg=m, o=tmp: e.tensor_scalar(
                o, o, sb["consts"][:, 8 + gg:9 + gg],
                sb["consts"][:, 16 + gg:17 + gg], ALU.mult, ALU.add),
                reads=[rtmp, "consts:ln_w", "consts:ln_b"], writes=[rtmp],
                tag=f"b_lwb_{tau}_{m}")
            old, rold = S32(ctx, 1024 + 512 * (m % 2), 512)
            P.emit("q", lambda e, mm=m, tt=tau, s=old: e.dma_start(
                out=s, in_=hbm[res_in].ap()[mm, :, TS * tt:TS * (tt + 1)]),
                reads=[f"{res_in}:{m}:{tau}"], writes=[rold], tag=f"b_old_{tau}_{m}")
            out, rout = S32(ctx, 2048 + 512 * (m % 2), 512)
            P.emit("v", lambda e, s=old, o=out, a=tmp: e.tensor_tensor(o, a, s, ALU.add),
                   reads=[rtmp, rold], writes=[rout], tag=f"b_radd_{tau}_{m}")
            P.emit("sd", lambda e, mm=m, tt=tau, s=out: e.dma_start(
                out=hbm[res_out].ap()[mm, :, TS * tt:TS * (tt + 1)], in_=s),
                reads=[rout], writes=[f"{res_out}:{m}:{tau}"], tag=f"b_rst_{tau}_{m}")
    return res_out



def kernel(**inputs):
    x = np.ascontiguousarray(np.asarray(inputs["L0_attn_a_hidden_states"], dtype=np.float32))
    wmap = {
        "Wq": inputs["L0_attn_a_Wq"], "Wk": inputs["L0_attn_a_Wk"],
        "Wv": inputs["L0_attn_a_Wv"], "Wg": inputs["L0_attn_a_Wg"],
        "Wo": inputs["L0_attn_a_Wo"], "gamma": inputs["L0_attn_a_gamma"],
        "beta": inputs["L0_attn_a_beta"], "W_in": inputs["L0_attn_b_W_in"],
        "conv_w": np.ascontiguousarray(np.asarray(inputs["L0_attn_b_conv_w"],
                                                  dtype=np.float32).reshape(D, 5)),
        "conv_b": inputs["L0_attn_b_conv_b"], "a_param": inputs["L0_attn_b_a_param"],
        "W_out": inputs["L0_attn_b_W_out"], "ln_w": inputs["L0_attn_b_ln_w"],
        "ln_b": inputs["L0_attn_b_ln_b"], "w_gate": inputs["L0_ffn_w_gate"],
        "w_up": inputs["L0_ffn_w_up"], "w_down": inputs["L0_ffn_w_down"],
    }
    wmap = {k: np.ascontiguousarray(np.asarray(v, dtype=np.float32)) for k, v in wmap.items()}
    nc = build_nc()
    in_maps = []
    for c in range(B):
        m = {"x_in": np.ascontiguousarray(x[c])}
        m.update(wmap)
        in_maps.append(m)
    res = run_bass_kernel_spmd(nc, in_maps, core_ids=list(range(B)))
    out = np.stack([res.results[c]["y_out"] for c in range(B)], axis=0)
    return out.astype(np.float32)
